# version 04
# baseline (speedup 1.0000x reference)
import contextlib
import math

import ml_dtypes
import numpy as np

import concourse.bass as bass
import concourse.mybir as mybir
from concourse.bass_utils import run_bass_kernel_spmd

F32 = mybir.dt.float32
BF16 = mybir.dt.bfloat16
AF = mybir.ActivationFunctionType
ALU = mybir.AluOpType
AX = mybir.AxisListType

B, S, D = 4, 4096, 2048
H, HD = 16, 128
FF = 8192
OWN = 2048
NQT = 4
EXT = [32, 24, 16, 8]
BLOCKS_EVEN = [7, 5, 2, 0]
BLOCKS_ODD = [6, 4, 3, 1]
SCALE = 1.0 / math.sqrt(HD)
EPS_LN = 1e-5
EPS_FF = 1e-6
QB = 128.0
CLIP_HI = 128.0 - 1e-6
ROPE_BASE = 10000.0
EXP_SHIFT = -30.0

_STAGE = 9
_P1SUB = 9
_DBG = False


class Ctr:
    def __init__(self, sem, step=1):
        self.sem, self.step, self.n = sem, step, 0

    def bump(self):
        self.n += self.step
        return self.n


def build_nc():
    nc = bass.Bass("TRN2", target_bir_lowering=False, detect_race_conditions=False)

    def inp(name, shape, dtype=F32):
        return nc.dram_tensor(name, list(shape), dtype, kind="ExternalInput").ap()

    x_full = inp("x_full", (S, D))
    x_own = inp("x_own", (OWN, D))
    wq = inp("wq", (128, 16 * D))
    wk = inp("wk", (128, 16 * D))
    wv = inp("wv", (128, 4 * FF))
    wo = inp("wo", (128, 64 * 512))
    wfc = inp("wfc", (16, 128, FF))
    wpj = inp("wpj", (64, 128, D))
    rk_sin = inp("rk_sin", (128, S), BF16)
    rk_cos = inp("rk_cos", (128, S), BF16)
    rq_sin = inp("rq_sin", (128, OWN), BF16)
    rq_cos = inp("rq_cos", (128, OWN), BF16)
    maskpack = inp("maskpack", (NQT, 128, 8 * 512), BF16)

    out = nc.dram_tensor("out", [OWN, D], F32, kind="ExternalOutput").ap()

    scr_kind = "ExternalOutput" if _DBG else "Internal"
    ln0_scr = nc.dram_tensor("ln0_scr", [8, 16, 128, 512], BF16, kind=scr_kind).ap()
    lnq_scr = nc.dram_tensor("lnq_scr", [4, 16, 128, 512], BF16, kind=scr_kind).ap()
    k_scr = nc.dram_tensor("k_scr", [H, 128, S], BF16, kind=scr_kind).ap()
    v_scr = nc.dram_tensor("v_scr", [H, 128, S], BF16, kind=scr_kind).ap()
    q_scr = nc.dram_tensor("q_scr", [H, 128, OWN], BF16, kind=scr_kind).ap()
    x1_scr = nc.dram_tensor("x1_scr", [OWN, D], F32, kind=scr_kind).ap()
    tfc_scr = nc.dram_tensor("tfc_scr", [16, 128, FF], BF16, kind=scr_kind).ap()
    tpj_scr = nc.dram_tensor("tpj_scr", [64, 128, D], BF16, kind=scr_kind).ap()
    if _DBG:
        av_dbg = nc.dram_tensor("av_dbg", [H, 128, OWN], BF16, kind="ExternalOutput").ap()
        gfin_dbg = nc.dram_tensor("gfin_dbg", [128, 10], F32, kind="ExternalOutput").ap()
        xh_dbg = nc.dram_tensor("xh_dbg", [16, 128, 512], BF16, kind="ExternalOutput").ap()
        g_dbg = nc.dram_tensor("g_dbg", [4, 128, FF], BF16, kind="ExternalOutput").ap()
        y_dbg = nc.dram_tensor("y_dbg", [16, 128, 512], BF16, kind="ExternalOutput").ap()
        x2_dbg = nc.dram_tensor("x2_dbg", [64, 128, 512], BF16, kind="ExternalOutput").ap()
        ytm_dbg = nc.dram_tensor("ytm_dbg", [4, 128, D], F32, kind="ExternalOutput").ap()

    ident_h = nc.inline_tensor(np.eye(128, dtype=ml_dtypes.bfloat16), "ident_h").ap()
    onesm_h = nc.inline_tensor(np.ones((128, 128), dtype=ml_dtypes.bfloat16), "onesm_h").ap()
    onesf_h = nc.inline_tensor(np.ones((128, 2), dtype=np.float32), "onesf_h").ap()
    onesr_h = nc.inline_tensor(np.ones((1, 128), dtype=np.float32), "onesr_h").ap()
    CONSTS = [EPS_LN, EPS_FF, EXP_SHIFT]
    consts_h = nc.inline_tensor(
        np.broadcast_to(np.asarray(CONSTS, np.float32), (128, len(CONSTS))).copy(),
        "consts_h").ap()

    SB_BASE, SB_TOP = 16384 + 128, 229376 - 64
    spans = []

    def at(name, shape, dtype, off):
        nbytes = int(np.prod(shape[1:])) * mybir.dt.size(dtype)
        assert off >= SB_BASE and off + nbytes <= SB_TOP, (name, off, nbytes)
        spans.append((off, off + nbytes, name))
        return nc.alloc_sbuf_tensor_at(name, list(shape), dtype, offset=off).ap()

    o = SB_BASE
    ident = at("ident", (128, 128), BF16, o); o += 256
    onesm = at("onesm", (128, 128), BF16, o); o += 256
    onesf = at("onesf", (128, 2), F32, o); o += 32
    onesr = at("onesr", (1, 128), F32, o); o += 512
    consts = at("consts", (128, len(CONSTS)), F32, o); o += 32
    stats = at("stats", (128, 96), F32, o); o += 384
    gacc = at("gacc", (128, 128), F32, o); o += 512
    gfin = at("gfin", (128, 16), F32, o); o += 64
    dump = at("dump", (128, 2048), BF16, o); o += 4096
    SMALL_END = o
    assert SMALL_END <= 24576

    for i, cval in enumerate(CONSTS):
        nc.const_aps.register(F32, cval, consts[0:128, i:i + 1])

    REG_A = 24576
    REG_B = REG_A + 131072
    assert REG_B + 65536 <= SB_TOP

    xbuf = [at(f"xbuf_{i}", (128, D), F32, REG_A + 98304 + i * 8192) for i in range(4)]
    wk_all = at("wk_all", (128, 16 * D), BF16, REG_A)
    wv_all = at("wv_all", (128, 4 * FF), BF16, REG_A)
    lblk = [at(f"lblk_{i}", (128, 16 * 512), BF16, REG_A + 65536 + i * 16384) for i in range(2)]
    kbuf = [at(f"kbuf_{i}", (128, S), BF16, REG_A + i * 8192) for i in range(2)]
    vbuf = [at(f"vbuf_{i}", (128, S), BF16, REG_A + 16384 + i * 8192) for i in range(2)]
    qbuf = [at(f"qbuf_{i}", (128, 512), BF16, REG_A + 32768 + i * 1024) for i in range(2)]
    masks = at("masks", (128, 8 * 512), BF16, REG_A + 34816)
    probs = [at(f"probs_{i}", (128, 512), BF16, REG_A + 43008 + i * 1024) for i in range(4)]
    linv = [at(f"linv_{i}", (128, 512), F32, REG_A + 47104 + i * 2048) for i in range(2)]
    twbuf = [at(f"twbuf_{i}", (128, D), F32, REG_A + 51200 + i * 8192) for i in range(2)]
    tpbuf = [at(f"tpbuf_{i}", (128, D), BF16, REG_A + 67584 + i * 4096) for i in range(2)]
    ttbuf = [at(f"ttbuf_{i}", (128, D), BF16, REG_A + 75776 + i * 4096) for i in range(2)]
    wo_all = at("wo_all", (128, 64 * 512), BF16, REG_A)
    ar2 = [at(f"ar2_{i}", (128, D), F32, REG_A + 65536 + i * 8192) for i in range(2)]
    xo_buf = [at(f"xo_{i}", (128, D), F32, REG_A + 81920 + i * 8192) for i in range(2)]
    ar_buf = [at(f"ar_{i}", (128, D), F32, REG_A + 98304 + i * 8192) for i in range(2)]
    gfm = [at(f"gfm_{i}", (128, 512), BF16, REG_A + i * 1024) for i in range(64)]
    x2fm = [at(f"x2fm_{i}", (128, 512), BF16, REG_A + i * 1024) for i in range(64)]
    dump6 = at("dump6", (128, 4096), BF16, REG_A)
    gtm = [at(f"gtm_{i}", (128, FF), BF16, REG_A + 65536 + i * 16384) for i in range(4)]
    xhstage = [at(f"xhst_{i}", (128, D), BF16, REG_A + 65536 + i * 4096) for i in range(4)]

    lnbuf = [at(f"lnbuf_{i}", (128, D), BF16, REG_B + i * 4096) for i in range(4)]
    gwbuf = [at(f"gwbuf_{i}", (128, D), F32, REG_B + 16384 + i * 8192) for i in range(2)]
    lnst = [at(f"lnst_{i}", (128, 512), BF16, REG_B + 59392 + i * 1024) for i in range(2)]
    tbl_sin = at("tbl_sin", (128, S), BF16, REG_B + 32768)
    tbl_cos = at("tbl_cos", (128, S), BF16, REG_B + 40960)
    kraw = [at(f"kraw_{i}", (128, 512), BF16, REG_B + 49152 + i * 1024) for i in range(4)]
    krot = [at(f"krot_{i}", (128, 512), BF16, REG_B + 53248 + i * 1024) for i in range(4)]
    ropetmp = at("ropetmp", (128, 1024), BF16, REG_B + 57344)
    vstg = [[at(f"vstg_{pb}_{h}", (128, 512), BF16, REG_B + 32768 + (pb * 16 + h) * 1024)
             for h in range(16)] for pb in range(2)]
    av = [at(f"av_{i}", (128, OWN), BF16, REG_B + i * 4096) for i in range(16)]
    xhfm = [at(f"xhfm_{i}", (128, 512), BF16, REG_B + i * 1024) for i in range(16)]
    yfm = [at(f"yfm_{i}", (128, 512), BF16, REG_B + i * 1024) for i in range(16)]
    wfbuf = [at(f"wfbuf_{i}", (128, 512), BF16, REG_B + 16384 + i * 1024) for i in range(6)]
    ytm = [at(f"ytm_{i}", (128, D), F32, REG_B + 22528 + i * 8192) for i in range(2)]
    x1buf = [at(f"x1b_{i}", (128, D), F32, REG_B + 38912 + i * 8192) for i in range(2)]

    PS = [nc.alloc_psum_tensor(f"ps{i}", [128, 512], F32).ap() for i in range(6)]
    PSB = [nc.alloc_psum_tensor(f"psb{i}", [128, 1024], BF16).ap() for i in range(2)]

    stack = contextlib.ExitStack()
    sems = {n: stack.enter_context(nc.semaphore(n)) for n in ("dy", "dg", "ct", "cv", "cs")}
    dy, dg = Ctr(sems["dy"], 16), Ctr(sems["dg"], 16)
    ct, cv, cs = Ctr(sems["ct"]), Ctr(sems["cv"]), Ctr(sems["cs"])

    OPS = {k: [] for k in ("tensor", "vector", "scalar", "sync", "gpsimd")}

    def dma(key, qc, dst, src):
        tok = qc.bump()
        sem = qc.sem
        OPS[key].append(lambda e, dst=dst, src=src, sem=sem:
                        e.dma_start(out=dst, in_=src).then_inc(sem, 16))
        return tok

    def wait(key, ctr, tok):
        if tok is None:
            return
        sem = ctr.sem
        OPS[key].append(lambda e, sem=sem, tok=tok: e.wait_ge(sem, tok))

    def inc(key, ctr, ob):
        tok = ctr.bump()
        sem = ctr.sem
        OPS[key].append(lambda e, ob=ob, sem=sem: ob(e).then_inc(sem, 1))
        return tok

    def run(key, ob):
        OPS[key].append(lambda e, ob=ob: ob(e))

    def vbar():
        tok = inc("vector", cv, lambda e: e.engine_nop())
        wait("vector", cv, tok)
        return tok

    def sbar():
        tok = inc("scalar", cs, lambda e: e.nop())
        wait("scalar", cs, tok)
        return tok

    dma("sync", dy, ident[:], ident_h)
    dma("sync", dy, onesm[:], onesm_h)
    dma("sync", dy, onesf[:], onesf_h)
    dma("sync", dy, onesr[:], onesr_h)
    t_consts = dma("sync", dy, consts[:], consts_h)
    for k in ("scalar", "vector", "tensor"):
        wait(k, dy, t_consts)

    def ln_stats(tiles, N, eps_val, base, in_toks=None, s_guard=None, splits=1, dmp=None):
        if dmp is None:
            dmp = dump
        nt = len(tiles)
        C = [[stats[:, base + 8 * t + j: base + 8 * t + j + 1] for j in range(8)]
             for t in range(nt)]
        if s_guard is not None:
            wait("scalar", cv, s_guard)
        sc_toks = []
        for t in range(nt):
            if in_toks and in_toks[t] is not None:
                wait("vector", in_toks[t][0], in_toks[t][1])
                wait("scalar", in_toks[t][0], in_toks[t][1])
            run("vector", lambda e, t=t: e.reduce_sum(out=C[t][0], in_=tiles[t], axis=AX.X))
            W = N // splits
            tk = None
            for sp in range(splits):
                acc = C[t][1] if sp == 0 else C[t][7]
                tk = inc("scalar", cs, lambda e, t=t, sp=sp, W=W, acc=acc, dmp=dmp: e.activation(
                    out=dmp[:, 0:W], in_=tiles[t][:, sp * W:(sp + 1) * W],
                    func=AF.Square, accum_out=acc))
            sc_toks.append(tk)
        vbar()
        for t in range(nt):
            wait("vector", cs, sc_toks[t])
            if splits == 2:
                run("vector", lambda e, t=t: e.tensor_tensor(
                    out=C[t][1], in0=C[t][1], in1=C[t][7], op=ALU.add))
        if splits == 2:
            vbar()
        for t in range(nt):
            run("vector", lambda e, t=t: e.tensor_scalar_mul(out=C[t][2], in0=C[t][0], scalar1=1.0 / N))
            run("vector", lambda e, t=t: e.tensor_scalar_mul(out=C[t][3], in0=C[t][1], scalar1=1.0 / N))
        vbar()
        for t in range(nt):
            run("vector", lambda e, t=t: e.tensor_tensor(out=C[t][4], in0=C[t][2], in1=C[t][2], op=ALU.mult))
        vbar()
        v_toks = [inc("vector", cv, lambda e, t=t: e.tensor_tensor(
            out=C[t][3], in0=C[t][3], in1=C[t][4], op=ALU.subtract)) for t in range(nt)]
        s_toks = []
        for t in range(nt):
            wait("scalar", cv, v_toks[t])
            s_toks.append(inc("scalar", cs, lambda e, t=t: e.activation(
                out=C[t][5], in_=C[t][3], func=AF.Sqrt, bias=eps_val)))
        res = []
        for t in range(nt):
            wait("vector", cs, s_toks[t])
            rt = inc("vector", cv, lambda e, t=t: e.reciprocal(out=C[t][6], in_=C[t][5]))
            res.append((C[t][2], C[t][6], rt))
        return res

    wfc_flat = wfc.rearrange("a p c -> (a p c)").rearrange("(n p c) -> n p c", p=128, c=2048)
    wpj_flat = wpj.rearrange("a p c -> (a p c)").rearrange("(n p c) -> n p c", p=128, c=2048)
    N_GCH = 128
    gchunk_i = [0]
    gw_used = {}

    def emit_gamma(k):
        if _P1SUB < 3:
            return
        for _ in range(k):
            ci = gchunk_i[0]
            if ci >= N_GCH:
                return
            gchunk_i[0] += 1
            slot = ci % 2
            if slot in gw_used:
                wait("gpsimd", cs, gw_used[slot])
            tok = dma("gpsimd", dg, gwbuf[slot][:],
                      wfc_flat[ci] if ci < 64 else wpj_flat[ci - 64])
            wait("scalar", dg, tok)
            gw_used[slot] = inc("scalar", cs, lambda e, slot=slot, ci=ci: e.activation(
                out=dump[:, 0:2048], in_=gwbuf[slot][:], func=AF.Abs,
                accum_out=gacc[:, ci:ci + 1]))

    gam_tok = None
    ln_done = None
    if _STAGE >= 1:
        xbuf_free = {}
        lngrp_tok = {}
        tr_i = [0]
        lnst_free = {}

        grp_tok = {}

        def ln_build(src_x, scr, ngrp, tag):
            for grp in range(ngrp):
                tiles, in_toks = [], []
                base = (grp % 2) * 40
                for j in range(4):
                    tc = grp * 4 + j
                    slot = tc % 4
                    if slot in xbuf_free:
                        wait("sync", cv, xbuf_free[slot])
                    tok = dma("sync", dy, xbuf[slot][:], src_x[tc * 128:(tc + 1) * 128, :])
                    tiles.append(xbuf[slot][:])
                    in_toks.append((dy, tok))
                st = ln_stats(tiles, D, EPS_LN, base, in_toks,
                              s_guard=lngrp_tok.get((tag, grp - 2)))
                for j in range(4):
                    wait("vector", cv, st[j][2])
                lt = None
                for j in range(4):
                    tc = grp * 4 + j
                    lt = inc("vector", cv, lambda e, j=j, st=st, tiles=tiles: e.tensor_scalar(
                        out=lnbuf[j][:], in0=tiles[j], scalar1=st[j][0], scalar2=st[j][1],
                        op0=ALU.subtract, op1=ALU.mult))
                    xbuf_free[tc % 4] = lt
                lngrp_tok[(tag, grp)] = lt
                emit_gamma(4)
                wait("tensor", cv, lt)
                for ic in range(16):
                    k = tr_i[0]
                    bank = PSB[k % 2]
                    sslot = k % 2
                    if k >= 2:
                        wait("tensor", dy, lnst_free[k - 2])
                    for j in range(3):
                        run("tensor", lambda e, ic=ic, j=j, bank=bank: e.transpose(
                            bank[:, j * 128:(j + 1) * 128],
                            lnbuf[j][:, ic * 128:(ic + 1) * 128], ident[:]))
                    ttok = inc("tensor", ct, lambda e, ic=ic, bank=bank: e.transpose(
                        bank[:, 3 * 128:4 * 128],
                        lnbuf[3][:, ic * 128:(ic + 1) * 128], ident[:]))
                    wait("vector", ct, ttok)
                    etok = inc("vector", cv, lambda e, sslot=sslot, bank=bank: e.tensor_copy(
                        out=lnst[sslot][:], in_=bank[:, 0:512]))
                    wait("sync", cv, etok)
                    lnst_free[k] = dma("sync", dy, scr[grp][ic], lnst[sslot][:])
                    grp_tok[(tag, grp)] = lnst_free[k]
                    tr_i[0] += 1

        ln_build(x_full, ln0_scr, 8 if _P1SUB >= 2 else 1, "f")
        if _P1SUB >= 2:
            ln_build(x_own, lnq_scr, 4, "q")
        ln_done = dy.n

        emit_gamma(N_GCH)
        if _P1SUB < 4:
            gam_tok = None
        for tok in gw_used.values():
            wait("vector", cs, tok)
    if _STAGE >= 1 and _P1SUB >= 4:
        run("vector", lambda e: e.reduce_sum(out=gfin[:, 0:1], in_=gacc[:, 0:64], axis=AX.X))
        run("vector", lambda e: e.reduce_sum(out=gfin[:, 1:2], in_=gacc[:, 64:128], axis=AX.X))
        gtok = vbar()
        wait("tensor", cv, gtok)
        t2 = inc("tensor", ct, lambda e: e.matmul(PS[5][0:1, 0:2], onesf[:, 0:1], gfin[:, 0:2],
                                                  start=True, stop=True))
        wait("vector", ct, t2)
        run("vector", lambda e: e.tensor_scalar_mul(out=gfin[0:1, 2:3], in0=PS[5][0:1, 0:1], scalar1=1.0 / (FF * D)))
        run("vector", lambda e: e.tensor_scalar_mul(out=gfin[0:1, 3:4], in0=PS[5][0:1, 1:2], scalar1=1.0 / (FF * D)))
        vbar()
        run("vector", lambda e: e.tensor_scalar(out=gfin[0:1, 4:5], in0=gfin[0:1, 2:3],
                                                scalar1=EPS_FF, scalar2=0.5, op0=ALU.add, op1=ALU.mult))
        run("vector", lambda e: e.tensor_scalar(out=gfin[0:1, 5:6], in0=gfin[0:1, 3:4],
                                                scalar1=EPS_FF, scalar2=0.5, op0=ALU.add, op1=ALU.mult))
        run("vector", lambda e: e.tensor_scalar_mul(out=gfin[0:1, 8:9], in0=gfin[0:1, 2:3], scalar1=-1.0))
        run("vector", lambda e: e.tensor_scalar_mul(out=gfin[0:1, 9:10], in0=gfin[0:1, 3:4], scalar1=-1.0))
        vbar()
        run("vector", lambda e: e.tensor_scalar_mul(out=gfin[0:1, 6:7], in0=gfin[0:1, 4:5], scalar1=-1.0))
        s5 = inc("vector", cv, lambda e: e.tensor_scalar_mul(out=gfin[0:1, 7:8], in0=gfin[0:1, 5:6], scalar1=-1.0))
        wait("tensor", cv, s5)
        t3 = inc("tensor", ct, lambda e: e.matmul(
            PS[5][:, 8:16], onesr[:], gfin[0:1, 2:10], start=True, stop=True))
        wait("vector", ct, t3)
        gam_tok = inc("vector", cv, lambda e: e.tensor_copy(out=gfin[:, 2:10], in_=PS[5][:, 8:16]))
        if _DBG:
            wait("sync", cv, gam_tok)
            dma("sync", dy, gfin_dbg, gfin[:, 0:10])

    if _STAGE >= 2:
        dma("sync", dy, tbl_sin[:], rk_sin)
        t_cos = dma("sync", dy, tbl_cos[:], rk_cos)

        def rope_phase(w_src, ntb, lsrc, scr, tbl_tok, tag):
            entry_ct, entry_cv = ct.n, cv.n
            wait("gpsimd", ct, entry_ct)
            wait("gpsimd", cv, entry_cv)
            wtok = dma("gpsimd", dg, wk_all[:], w_src)
            wait("tensor", dg, wtok)
            wait("vector", dy, tbl_tok)
            bank_ev, kraw_used, krot_stored, lblk_done = {}, {}, {}, {}
            for tb in range(ntb):
                bslot = tb % 2
                if tb >= 2:
                    wait("sync", ct, lblk_done[tb - 2])
                else:
                    wait("sync", ct, entry_ct)
                wait("sync", dy, grp_tok[(tag, tb)])
                ltok = dma("sync", dy, lblk[bslot][:].rearrange("p (a c) -> p a c", c=512),
                           lsrc[tb].rearrange("a p c -> p a c"))
                wait("tensor", dy, ltok)
                last = None
                for oc in range(16):
                    it = tb * 16 + oc
                    bank = PS[it % 4]
                    if it - 4 in bank_ev:
                        wait("tensor", cs, bank_ev[it - 4])
                    for ic in range(15):
                        run("tensor", lambda e, oc=oc, ic=ic, bslot=bslot, bank=bank: e.matmul(
                            bank[:], wk_all[:, oc * D + ic * 128: oc * D + (ic + 1) * 128],
                            lblk[bslot][:, ic * 512:(ic + 1) * 512],
                            start=(ic == 0), stop=False))
                    last = inc("tensor", ct, lambda e, oc=oc, bslot=bslot, bank=bank: e.matmul(
                        bank[:], wk_all[:, oc * D + 15 * 128: oc * D + 16 * 128],
                        lblk[bslot][:, 15 * 512:16 * 512],
                        start=False, stop=True))
                    kslot = it % 4
                    if it - 4 in kraw_used:
                        wait("scalar", cv, kraw_used[it - 4])
                    wait("scalar", ct, last)
                    ev = inc("scalar", cs, lambda e, kslot=kslot, bank=bank: e.copy(
                        out=kraw[kslot][:], in_=bank[:]))
                    bank_ev[it] = ev
                    rslot = it % 4
                    if it - 4 in krot_stored:
                        wait("vector", dy, krot_stored[it - 4])
                    wait("vector", cs, ev)
                    c0 = tb * 512
                    kr, ko = kraw[kslot], krot[rslot]
                    run("vector", lambda e, kr=kr, c0=c0: e.tensor_tensor(
                        out=ropetmp[0:64, 0:512], in0=kr[0:64, :], in1=tbl_sin[0:64, c0:c0 + 512], op=ALU.mult))
                    run("vector", lambda e, kr=kr, c0=c0: e.tensor_tensor(
                        out=ropetmp[0:64, 512:1024], in0=kr[64:128, :], in1=tbl_cos[64:128, c0:c0 + 512], op=ALU.mult))
                    run("vector", lambda e, kr=kr, c0=c0: e.tensor_tensor(
                        out=ropetmp[64:128, 0:512], in0=kr[0:64, :], in1=tbl_cos[0:64, c0:c0 + 512], op=ALU.mult))
                    run("vector", lambda e, kr=kr, c0=c0: e.tensor_tensor(
                        out=ropetmp[64:128, 512:1024], in0=kr[64:128, :], in1=tbl_sin[64:128, c0:c0 + 512], op=ALU.mult))
                    vbar()
                    run("vector", lambda e, ko=ko: e.tensor_tensor(
                        out=ko[0:64, :], in0=ropetmp[0:64, 0:512], in1=ropetmp[0:64, 512:1024], op=ALU.subtract))
                    rt = inc("vector", cv, lambda e, ko=ko: e.tensor_tensor(
                        out=ko[64:128, :], in0=ropetmp[64:128, 0:512], in1=ropetmp[64:128, 512:1024], op=ALU.add))
                    kraw_used[it] = rt
                    wait("sync", cv, rt)
                    krot_stored[it] = dma("sync", dy, scr[oc][:, c0:c0 + 512], ko[:])
                lblk_done[tb] = last

        rope_phase(wk, 8, ln0_scr, k_scr, t_cos, "f")
        wait("sync", cv, cv.n)
        dma("sync", dy, tbl_sin[:, 0:OWN], rq_sin)
        t_qc = dma("sync", dy, tbl_cos[:, 0:OWN], rq_cos)
        rope_phase(wq, 4, lnq_scr, q_scr, t_qc, "q")

        wait("gpsimd", ct, ct.n)
        wvtok = dma("gpsimd", dg, wv_all[:], wv)
        wait("tensor", dg, wvtok)
        v_entry_cv = cv.n
        v_entry_dy = dy.n
        ev_last = {}
        vstg_stored = {}
        lblk_done_v = {}
        for tb in range(8):
            bslot = tb % 2
            pb = tb % 2
            if tb >= 2:
                wait("sync", ct, lblk_done_v[tb - 2])
            else:
                wait("sync", ct, ct.n)
            ltok = dma("sync", dy, lblk[bslot][:].rearrange("p (a c) -> p a c", c=512),
                       ln0_scr[tb].rearrange("a p c -> p a c"))
            wait("tensor", dy, ltok)
            if tb < 2:
                wait("vector", cv, v_entry_cv)
                wait("vector", dy, v_entry_dy)
            else:
                for h in range(16):
                    wait("vector", dy, vstg_stored[(tb - 2, h)])
            mt = None
            for twi in range(4):
                t128 = tb * 4 + twi
                for ogp in range(2):
                    gi = t128 * 2 + ogp
                    b0, b1 = PS[(gi % 3) * 2], PS[(gi % 3) * 2 + 1]
                    if gi - 3 in ev_last:
                        wait("tensor", cv, ev_last[gi - 3])
                    mt = None
                    for ic in range(16):
                        for oh, bk in ((0, b0), (1, b1)):
                            og = ogp * 2 + oh
                            mmf = (lambda e, ic=ic, twi=twi, og=og, bslot=bslot, bk=bk: e.matmul(
                                bk[:],
                                lblk[bslot][:, ic * 512 + twi * 128: ic * 512 + (twi + 1) * 128],
                                wv_all[:, og * FF + ic * 512: og * FF + (ic + 1) * 512],
                                start=(ic == 0), stop=(ic == 15), skip_group_check=True))
                            if ic == 15 and oh == 1:
                                mt = inc("tensor", ct, mmf)
                            else:
                                run("tensor", mmf)
                    wait("vector", ct, mt)
                    et = None
                    for oh, bk in ((0, b0), (1, b1)):
                        og = ogp * 2 + oh
                        for hh in range(4):
                            h = og * 4 + hh
                            et = inc("vector", cv, lambda e, hh=hh, h=h, twi=twi, pb=pb, bk=bk: e.tensor_copy(
                                out=vstg[pb][h][:, twi * 128:(twi + 1) * 128],
                                in_=bk[:, hh * 128:(hh + 1) * 128]))
                    ev_last[gi] = et
            lblk_done_v[tb] = mt
            for h in range(16):
                wait("sync", cv, ev_last[(tb * 4 + 3) * 2 + 1])
                vstg_stored[(tb, h)] = dma("sync", dy, v_scr[h][:, tb * 512:(tb + 1) * 512],
                                           vstg[pb][h][:])

    tern_i = [0]
    tw_used = {}
    tp_done = {}

    def emit_tern(k):
        if gam_tok is None or _STAGE < 3:
            return
        for _ in range(k):
            ci = tern_i[0]
            if ci >= N_GCH:
                return
            tern_i[0] += 1
            slot = ci % 2
            if slot in tw_used:
                wait("gpsimd", cv, tw_used[slot])
            if ci < 64:
                tsrc = wfc[ci // 4][:, (ci % 4) * 2048:(ci % 4 + 1) * 2048]
            else:
                tsrc = wpj[ci - 64]
            tok = dma("gpsimd", dg, twbuf[slot][:], tsrc)
            thr = gfin[:, 4:5] if ci < 64 else gfin[:, 5:6]
            nthr = gfin[:, 6:7] if ci < 64 else gfin[:, 7:8]
            if ci == 0:
                wait("vector", cv, gam_tok)
            if slot in tp_done:
                wait("vector", dy, tp_done[slot])
            wait("vector", dg, tok)
            run("vector", lambda e, slot=slot, thr=thr: e.tensor_single_scalar(
                out=tpbuf[slot][:], in_=twbuf[slot][:], scalar=thr, op=ALU.is_ge))
            vbar()
            tw_used[slot] = inc("vector", cv, lambda e, slot=slot, nthr=nthr: e.scalar_tensor_tensor(
                out=ttbuf[slot][:], in0=twbuf[slot][:], scalar=nthr, in1=tpbuf[slot][:],
                op0=ALU.is_le, op1=ALU.subtract))
            wait("sync", cv, tw_used[slot])
            if ci < 64:
                dst = tfc_scr[ci // 4][:, (ci % 4) * 2048:(ci % 4 + 1) * 2048]
            else:
                dst = tpj_scr[ci - 64][:]
            tp_done[slot] = dma("sync", dy, dst, ttbuf[slot][:])

    av_evac = {}
    if _STAGE >= 3:
        wait("sync", ct, ct.n)
        wait("sync", cv, cv.n)
        wait("gpsimd", ct, ct.n)
        wait("gpsimd", cv, cv.n)
        if gam_tok is not None:
            wait("tensor", cv, gam_tok)
        mask_used = None
        kvq_free = {}
        recip_tok = {}
        for qt in range(NQT):
            ext = EXT[qt]
            if mask_used is not None:
                wait("sync", cv, mask_used)
            mask_tok = dma("sync", dy, masks[:], maskpack[qt])
            for h in range(H):
                ih = qt * H + h
                slot = ih % 2
                if (ih - 2, slot) in kvq_free:
                    wait("sync", ct, kvq_free[(ih - 2, slot)])
                ktok = dma("sync", dy, kbuf[slot][:, 0:ext * 128], k_scr[h][:, 0:ext * 128])
                vtok = dma("sync", dy, vbuf[slot][:, 0:ext * 128], v_scr[h][:, 0:ext * 128])
                qtok = dma("sync", dy, qbuf[slot][:], q_scr[h][:, qt * 512:(qt + 1) * 512])
                wait("tensor", dy, qtok)
                emit_tern(2)
                avb, lb = PS[ih % 2], PS[5]
                if ih - 2 in av_evac:
                    wait("tensor", cv, av_evac[ih - 2])
                if ih - 1 in recip_tok:
                    wait("tensor", cv, recip_tok[ih - 1])
                exp_toks, pb_free = {}, {}
                last_l = None
                for kc in range(ext):
                    bank = PS[2 + kc % 3]
                    if kc - 3 in exp_toks:
                        wait("tensor", cs, exp_toks[kc - 3])
                    stok = inc("tensor", ct, lambda e, kc=kc, slot=slot, bank=bank: e.matmul(
                        bank[:], kbuf[slot][:, kc * 128:(kc + 1) * 128], qbuf[slot][:],
                        start=True, stop=True))
                    pslot = kc % 4
                    wait("scalar", ct, stok)
                    if kc - 4 in pb_free:
                        wait("scalar", ct, pb_free[kc - 4])
                    ev = inc("scalar", cs, lambda e, pslot=pslot, bank=bank: e.activation(
                        out=probs[pslot][:], in_=bank[:], func=AF.Exp, scale=SCALE,
                        bias=EXP_SHIFT))
                    exp_toks[kc] = ev
                    if kc >= ext - 8:
                        pos = kc - (ext - 8)
                        if kc == ext - 8:
                            wait("vector", dy, mask_tok)
                        wait("vector", cs, ev)
                        mk = inc("vector", cv, lambda e, pslot=pslot, pos=pos: e.tensor_tensor(
                            out=probs[pslot][:], in0=probs[pslot][:],
                            in1=masks[:, pos * 512:(pos + 1) * 512], op=ALU.mult))
                        wait("tensor", cv, mk)
                    else:
                        wait("tensor", cs, ev)
                    run("tensor", lambda e, kc=kc, slot=slot, pslot=pslot, avb=avb, ext=ext: e.matmul(
                        avb[:], vbuf[slot][:, kc * 128:(kc + 1) * 128], probs[pslot][:],
                        start=(kc == 0), stop=(kc == ext - 1), skip_group_check=True))
                    last_l = inc("tensor", ct, lambda e, kc=kc, pslot=pslot, lb=lb, ext=ext: e.matmul(
                        lb[:], onesm[:], probs[pslot][:],
                        start=(kc == 0), stop=(kc == ext - 1), skip_group_check=True))
                    pb_free[kc] = last_l
                kvq_free[(ih, slot)] = last_l
                if h == H - 1:
                    mask_used = cv.n
                lslot = ih % 2
                wait("vector", ct, last_l)
                recip_tok[ih] = inc("vector", cv, lambda e, lslot=lslot, lb=lb: e.reciprocal(
                    out=linv[lslot][:], in_=lb[:]))
                vbar()
                av_evac[ih] = inc("vector", cv, lambda e, h=h, qt=qt, lslot=lslot, avb=avb: e.tensor_tensor(
                    out=av[h][:, qt * 512:(qt + 1) * 512], in0=avb[:], in1=linv[lslot][:], op=ALU.mult))
        emit_tern(N_GCH)
        if _DBG:
            for h in range(H):
                wait("sync", cv, av_evac[(NQT - 1) * H + h])
                dma("sync", dy, av_dbg[h][:], av[h][:])

    x1_stored = {}
    if _STAGE >= 4:
        wait("gpsimd", ct, ct.n)
        wait("gpsimd", cv, cv.n)
        wait("vector", dy, dy.n)
        wait("tensor", cs, cs.n)
        wait("tensor", cv, cv.n)
        wo_tok = dma("gpsimd", dg, wo_all[:], wo)
        wait("tensor", dg, wo_tok)
        prev_ev = {}
        for tc in range(16):
            slot = tc % 2
            if ("v", tc - 2) in x1_stored:
                wait("sync", cv, x1_stored[("v", tc - 2)])
            xo_tok = dma("sync", dy, xo_buf[slot][:], x_own[tc * 128:(tc + 1) * 128, :])
            banks = [PS[g] for g in range(4)]
            if tc >= 1:
                for g in range(4):
                    wait("tensor", cs, prev_ev[(tc - 1, g)])
            mtok = None
            for hh in range(H):
                for g in range(4):
                    mmf = (lambda e, hh=hh, g=g, tc=tc, banks=banks: e.matmul(
                        banks[g][:], av[hh][:, tc * 128:(tc + 1) * 128],
                        wo_all[:, (hh * 4 + g) * 512:(hh * 4 + g + 1) * 512],
                        start=(hh == 0), stop=(hh == H - 1), skip_group_check=True))
                    if hh == H - 1 and g == 3:
                        mtok = inc("tensor", ct, mmf)
                    else:
                        run("tensor", mmf)
            wait("scalar", ct, mtok)
            if ("v", tc - 2) in x1_stored:
                wait("scalar", cv, x1_stored[("v", tc - 2)])
            for g in range(4):
                prev_ev[(tc, g)] = inc("scalar", cs, lambda e, g=g, slot=slot, banks=banks: e.copy(
                    out=ar_buf[slot][:, g * 512:(g + 1) * 512], in_=banks[g][:]))
            base = 80 + (tc % 2) * 8
            st = ln_stats([ar_buf[slot][:]], D, EPS_LN, base,
                          in_toks=[(cs, prev_ev[(tc, 3)])],
                          s_guard=x1_stored.get(("v", tc - 2)))
            mean, rstd, rtok = st[0]
            wait("vector", cv, rtok)
            run("vector", lambda e, slot=slot, mean=mean, rstd=rstd: e.tensor_scalar(
                out=ar2[slot][:], in0=ar_buf[slot][:], scalar1=mean, scalar2=rstd,
                op0=ALU.subtract, op1=ALU.mult))
            vbar()
            wait("vector", dy, xo_tok)
            vt = inc("vector", cv, lambda e, slot=slot: e.tensor_tensor(
                out=ar2[slot][:], in0=ar2[slot][:], in1=xo_buf[slot][:], op=ALU.add))
            x1_stored[("v", tc)] = vt
            wait("sync", cv, vt)
            x1_stored[tc] = dma("sync", dy, x1_scr[tc * 128:(tc + 1) * 128, :], ar2[slot][:])

    if _STAGE >= 5:
        emit_tern(N_GCH)
        last_tp = max(tp_done.values()) if tp_done else None
        prev = {}
        x1slot_free = {}
        wait("sync", ct, ct.n)
        wait("sync", cv, x1_stored[("v", 15)])
        wait("vector", dy, x1_stored[15])
        wait("tensor", cs, cs.n)
        for tb in range(NQT):
            qtoks = []
            for tc2 in range(4):
                tci = tb * 4 + tc2
                slot = tc2 % 2
                if slot in x1slot_free:
                    wait("sync", cv, x1slot_free[slot])
                x1tok = dma("sync", dy, x1buf[slot][:], x1_scr[tci * 128:(tci + 1) * 128, :])
                base = (tc2 % 2) * 8
                st = ln_stats([x1buf[slot][:]], D, EPS_FF, base, in_toks=[(dy, x1tok)],
                              s_guard=prev.get(("q", tb, tc2 - 2),
                                               x1_stored[("v", 15)] if tb == 0 else None))
                mean, rstd, rtok = st[0]
                wait("vector", cv, rtok)
                if ("yt", slot) in prev:
                    wait("vector", dy, prev[("yt", slot)])
                run("vector", lambda e, slot=slot, mean=mean, rstd=rstd: e.tensor_scalar(
                    out=ytm[slot][:], in0=x1buf[slot][:], scalar1=mean, scalar2=rstd,
                    op0=ALU.subtract, op1=ALU.mult))
                vbar()
                qb1 = 16 + (tc2 % 2) * 4
                g1 = stats[:, qb1:qb1 + 1]
                rg1 = stats[:, qb1 + 1:qb1 + 2]
                c1 = stats[:, qb1 + 2:qb1 + 3]
                c2 = stats[:, qb1 + 3:qb1 + 4]
                run("vector", lambda e, slot=slot, g1=g1: e.reduce_max(
                    out=g1, in_=ytm[slot][:], axis=AX.X, apply_absolute_value=True))
                vbar()
                run("vector", lambda e, g1=g1, rg1=rg1: e.reciprocal(out=rg1, in_=g1))
                vbar()
                run("vector", lambda e, rg1=rg1, c1=c1: e.tensor_scalar_mul(out=c1, in0=rg1, scalar1=QB))
                run("vector", lambda e, g1=g1, c2=c2: e.tensor_scalar_mul(out=c2, in0=g1, scalar1=1.0 / QB))
                vbar()
                run("vector", lambda e, slot=slot, c1=c1: e.tensor_scalar(
                    out=ytm[slot][:], in0=ytm[slot][:], scalar1=c1, scalar2=CLIP_HI,
                    op0=ALU.mult, op1=ALU.min))
                vbar()
                xq = inc("vector", cv, lambda e, slot=slot, tc2=tc2, c2=c2: e.tensor_scalar(
                    out=xhstage[tc2][:], in0=ytm[slot][:], scalar1=-CLIP_HI, scalar2=c2,
                    op0=ALU.max, op1=ALU.mult))
                prev[("q", tb, tc2)] = xq
                x1slot_free[slot] = xq
                qtoks.append(xq)
            for tok in qtoks:
                wait("tensor", cv, tok)
            tr_ev = {}
            for ic in range(16):
                bank = PSB[ic % 2]
                if ic - 2 in tr_ev:
                    wait("tensor", cv, tr_ev[ic - 2])
                for j in range(3):
                    run("tensor", lambda e, ic=ic, j=j, bank=bank: e.transpose(
                        bank[:, j * 128:(j + 1) * 128],
                        xhstage[j][:, ic * 128:(ic + 1) * 128], ident[:]))
                tt = inc("tensor", ct, lambda e, ic=ic, bank=bank: e.transpose(
                    bank[:, 3 * 128:4 * 128],
                    xhstage[3][:, ic * 128:(ic + 1) * 128], ident[:]))
                wait("vector", ct, tt)
                tr_ev[ic] = inc("vector", cv, lambda e, ic=ic, bank=bank: e.tensor_copy(
                    out=xhfm[ic][:], in_=bank[:, 0:512]))

            if _DBG and tb == 0:
                for ic in range(16):
                    wait("sync", cv, tr_ev[ic])
                    dma("sync", dy, xh_dbg[ic], xhfm[ic][:])
            if tb == 0 and last_tp is not None:
                wait("sync", dy, last_tp)
            for ic in range(16):
                wait("tensor", cv, tr_ev[ic])
            fc_ev = {}
            for og in range(16):
                banks = [PS[b] for b in range(4)]
                if og >= 1:
                    for b in range(4):
                        wait("tensor", cs, fc_ev[(og - 1, b)])
                for i in range(16):
                    wslot = (og * 16 + i) % 2
                    if ("w", wslot) in prev:
                        wait("sync", ct, prev[("w", wslot)])
                    wtok = dma("sync", dy, wfbuf[wslot][:], tfc_scr[i][:, og * 512:(og + 1) * 512])
                    wait("tensor", dy, wtok)
                    for b in range(3):
                        run("tensor", lambda e, i=i, b=b, wslot=wslot, banks=banks: e.matmul(
                            banks[b][:], wfbuf[wslot][:, b * 128:(b + 1) * 128], xhfm[i][:],
                            start=(i == 0), stop=(i == 15), skip_group_check=True))
                    prev[("w", wslot)] = inc("tensor", ct, lambda e, i=i, wslot=wslot, banks=banks: e.matmul(
                        banks[3][:], wfbuf[wslot][:, 3 * 128:4 * 128], xhfm[i][:],
                        start=(i == 0), stop=(i == 15), skip_group_check=True))
                mtok = prev[("w", (og * 16 + 15) % 2)]
                wait("scalar", ct, mtok)
                if og == 0 and tb == 0:
                    wait("scalar", cv, gam_tok)
                for b in range(4):
                    fc_ev[(og, b)] = inc("scalar", cs, lambda e, og=og, b=b, banks=banks: e.activation(
                        out=gfm[og * 4 + b][:], in_=banks[b][:], func=AF.Gelu,
                        scale=gfin[:, 8:9]))

            g_ev = {}
            for obg in range(16):
                for b in range(4):
                    wait("tensor", cs, fc_ev[(obg, b)])
                for tc2 in range(4):
                    k = obg * 4 + tc2
                    bank = PSB[k % 2]
                    if k - 2 in g_ev:
                        wait("tensor", cv, g_ev[k - 2])
                    for j in range(3):
                        run("tensor", lambda e, obg=obg, j=j, tc2=tc2, bank=bank: e.transpose(
                            bank[:, j * 128:(j + 1) * 128],
                            gfm[obg * 4 + j][:, tc2 * 128:(tc2 + 1) * 128], ident[:]))
                    tt = inc("tensor", ct, lambda e, obg=obg, tc2=tc2, bank=bank: e.transpose(
                        bank[:, 3 * 128:4 * 128],
                        gfm[obg * 4 + 3][:, tc2 * 128:(tc2 + 1) * 128], ident[:]))
                    wait("vector", ct, tt)
                    g_ev[k] = inc("vector", cv, lambda e, tc2=tc2, obg=obg, bank=bank: e.tensor_copy(
                        out=gtm[tc2][:, obg * 512:(obg + 1) * 512], in_=bank[:, 0:512]))

            if _DBG and tb == 0:
                ga = max(g_ev.values())
                for tc2 in range(4):
                    wait("sync", cv, ga)
                    dma("sync", dy, g_dbg[tc2], gtm[tc2][:])
                _gdbg_tok = dy.n
                wait("vector", dy, _gdbg_tok)
            g_all = max(g_ev.values())
            wait("vector", cv, g_all)
            wait("scalar", cv, g_all)
            st4 = ln_stats([gtm[t][:] for t in range(4)], FF, EPS_FF, 24,
                           s_guard=prev.get(("x2", tb - 1)), splits=2, dmp=dump6)
            for t in range(4):
                wait("vector", cv, st4[t][2])
            for t in range(4):
                run("vector", lambda e, t=t, st4=st4: e.tensor_scalar(
                    out=gtm[t][:], in0=gtm[t][:], scalar1=st4[t][0], scalar2=st4[t][1],
                    op0=ALU.subtract, op1=ALU.mult))
            vbar()
            QC = [[stats[:, 56 + 4 * t + j:56 + 4 * t + j + 1] for j in range(4)]
                  for t in range(4)]
            for t in range(4):
                run("vector", lambda e, t=t, QC=QC: e.reduce_max(
                    out=QC[t][0], in_=gtm[t][:], axis=AX.X, apply_absolute_value=True))
            vbar()
            for t in range(4):
                run("vector", lambda e, t=t, QC=QC: e.reciprocal(out=QC[t][1], in_=QC[t][0]))
            vbar()
            for t in range(4):
                run("vector", lambda e, t=t, QC=QC: e.tensor_scalar_mul(out=QC[t][2], in0=QC[t][1], scalar1=QB))
                run("vector", lambda e, t=t, QC=QC: e.tensor_scalar_mul(out=QC[t][3], in0=QC[t][0], scalar1=1.0 / QB))
            vbar()
            for t in range(4):
                run("vector", lambda e, t=t, QC=QC: e.tensor_scalar(
                    out=gtm[t][:], in0=gtm[t][:], scalar1=QC[t][2], scalar2=CLIP_HI,
                    op0=ALU.mult, op1=ALU.min))
            vbar()
            x2toks = {}
            for t in range(4):
                x2toks[t] = inc("vector", cv, lambda e, t=t, QC=QC: e.tensor_scalar(
                    out=gtm[t][:], in0=gtm[t][:], scalar1=-CLIP_HI, scalar2=QC[t][3],
                    op0=ALU.max, op1=ALU.mult))
            prev[("x2", tb)] = x2toks[3]

            for tc2 in range(4):
                wait("tensor", cv, x2toks[tc2])
            x2_ev = {}
            for ob in range(64):
                bank = PSB[ob % 2]
                if ob - 2 in x2_ev:
                    wait("tensor", cv, x2_ev[ob - 2])
                for tc2 in range(3):
                    run("tensor", lambda e, ob=ob, tc2=tc2, bank=bank: e.transpose(
                        bank[:, tc2 * 128:(tc2 + 1) * 128],
                        gtm[tc2][:, ob * 128:(ob + 1) * 128], ident[:]))
                tt = inc("tensor", ct, lambda e, ob=ob, bank=bank: e.transpose(
                    bank[:, 3 * 128:4 * 128],
                    gtm[3][:, ob * 128:(ob + 1) * 128], ident[:]))
                wait("vector", ct, tt)
                x2_ev[ob] = inc("vector", cv, lambda e, ob=ob, bank=bank: e.tensor_copy(
                    out=x2fm[ob][:], in_=bank[:, 0:512]))

            if _DBG and tb == 0:
                for ob in range(64):
                    wait("sync", cv, x2_ev[ob])
                    dma("sync", dy, x2_dbg[ob], x2fm[ob][:])
            for ob in range(64):
                wait("tensor", cv, x2_ev[ob])
            pj_ev = {}
            for b in range(4):
                wait("tensor", cs, fc_ev[(15, b)])
            for og in range(4):
                banks = [PS[b] for b in range(4)]
                if og >= 1:
                    for b in range(4):
                        wait("tensor", cs, pj_ev[(og - 1, b)])
                for i in range(64):
                    wslot = (og * 64 + i) % 2
                    wait("sync", ct, prev[("w", wslot)])
                    wtok = dma("sync", dy, wfbuf[wslot][:], tpj_scr[i][:, og * 512:(og + 1) * 512])
                    wait("tensor", dy, wtok)
                    for b in range(3):
                        run("tensor", lambda e, i=i, b=b, wslot=wslot, banks=banks: e.matmul(
                            banks[b][:], wfbuf[wslot][:, b * 128:(b + 1) * 128], x2fm[i][:],
                            start=(i == 0), stop=(i == 63), skip_group_check=True))
                    prev[("w", wslot)] = inc("tensor", ct, lambda e, i=i, wslot=wslot, banks=banks: e.matmul(
                        banks[3][:], wfbuf[wslot][:, 3 * 128:4 * 128], x2fm[i][:],
                        start=(i == 0), stop=(i == 63), skip_group_check=True))
                mtok = prev[("w", (og * 64 + 63) % 2)]
                wait("scalar", ct, mtok)
                for b in range(4):
                    pj_ev[(og, b)] = inc("scalar", cs, lambda e, og=og, b=b, banks=banks: e.activation(
                        out=yfm[og * 4 + b][:], in_=banks[b][:], func=AF.Copy,
                        scale=gfin[:, 9:10]))

            if _DBG and tb == 0:
                for oc in range(16):
                    wait("sync", cs, pj_ev[(oc // 4, oc % 4)])
                    dma("sync", dy, y_dbg[oc], yfm[oc][:])
                wait("tensor", dy, dy.n)
            for og in range(4):
                for b in range(4):
                    wait("tensor", cs, pj_ev[(og, b)])
            y_ev = {}
            for tc2 in range(4):
                tci = tb * 4 + tc2
                slot = tc2 % 2
                ytoks = []
                for ocg in range(4):
                    kk = tc2 * 4 + ocg
                    bank = PSB[ocg % 2]
                    if kk - 2 in y_ev:
                        wait("tensor", cv, y_ev[kk - 2])
                    for j in range(3):
                        oc = ocg * 4 + j
                        run("tensor", lambda e, oc=oc, j=j, tc2=tc2, bank=bank: e.transpose(
                            bank[:, j * 128:(j + 1) * 128],
                            yfm[oc][:, tc2 * 128:(tc2 + 1) * 128], ident[:]))
                    tt = inc("tensor", ct, lambda e, ocg=ocg, tc2=tc2, bank=bank: e.transpose(
                        bank[:, 3 * 128:4 * 128],
                        yfm[ocg * 4 + 3][:, tc2 * 128:(tc2 + 1) * 128], ident[:]))
                    wait("vector", ct, tt)
                    if ("yt", slot) in prev and ocg == 0:
                        wait("vector", dy, prev[("yt", slot)])
                    ytoks.append(inc("vector", cv, lambda e, ocg=ocg, slot=slot, bank=bank: e.tensor_copy(
                        out=ytm[slot][:, ocg * 512:(ocg + 1) * 512], in_=bank[:, 0:512])))
                    y_ev[kk] = ytoks[-1]
                if _DBG and tb == 0:
                    wait("sync", cv, ytoks[3])
                    ydtok = dma("sync", dy, ytm_dbg[tc2], ytm[slot][:])
                    wait("vector", dy, ydtok)
                base = 72 + (tc2 % 2) * 8
                st = ln_stats([ytm[slot][:]], D, EPS_LN, base,
                              in_toks=[(cv, ytoks[3])], s_guard=prev.get(("out", tc2 - 2, tb)))
                mean, rstd, rtok = st[0]
                if slot in x1slot_free:
                    wait("sync", cv, x1slot_free[slot])
                x1tok = dma("sync", dy, x1buf[slot][:], x1_scr[tci * 128:(tci + 1) * 128, :])
                wait("vector", cv, rtok)
                run("vector", lambda e, slot=slot, mean=mean, rstd=rstd: e.tensor_scalar(
                    out=ytm[slot][:], in0=ytm[slot][:], scalar1=mean, scalar2=rstd,
                    op0=ALU.subtract, op1=ALU.mult))
                vbar()
                wait("vector", dy, x1tok)
                vt = inc("vector", cv, lambda e, slot=slot: e.tensor_tensor(
                    out=ytm[slot][:], in0=ytm[slot][:], in1=x1buf[slot][:], op=ALU.add))
                prev[("out", tc2, tb)] = vt
                x1slot_free[slot] = vt
                wait("sync", cv, vt)
                prev[("yt", slot)] = dma("sync", dy, out[tci * 128:(tci + 1) * 128, :], ytm[slot][:])

    wait("sync", dy, dy.n)
    wait("gpsimd", dg, dg.n)

    with nc.Block() as block:
        @block.tensor
        def _(e):
            for f in OPS["tensor"]:
                f(e)

        @block.vector
        def _(e):
            for f in OPS["vector"]:
                f(e)

        @block.scalar
        def _(e):
            for f in OPS["scalar"]:
                f(e)

        @block.sync
        def _(e):
            for f in OPS["sync"]:
                f(e)

        @block.gpsimd
        def _(e):
            for f in OPS["gpsimd"]:
                f(e)

    stack.close()
    return nc


def _to_bf16(a):
    return np.ascontiguousarray(a.astype(ml_dtypes.bfloat16))


def _rope_tables(positions):
    inv = ROPE_BASE ** (-(np.arange(0, HD, 2, dtype=np.float64) / HD))
    freqs = np.asarray(positions, np.float64)[None, :] * inv[:, None]
    s, c = np.sin(freqs), np.cos(freqs)
    return (_to_bf16(np.concatenate([s, s], 0)), _to_bf16(np.concatenate([c, c], 0)))


def _maskpack_for(blocks):
    mp = np.zeros((NQT, 8, 128, 512), dtype=np.float32)
    for qt, blk in enumerate(blocks):
        ext = EXT[qt]
        q0 = blk * 512
        for pos in range(8):
            kc = ext - 8 + pos
            k0 = kc * 128
            if k0 + 127 < q0:
                mp[qt, pos] = 1.0
            elif k0 > q0 + 511:
                mp[qt, pos] = 0.0
            else:
                kk = k0 + np.arange(128)[:, None]
                qq = q0 + np.arange(512)[None, :]
                mp[qt, pos] = (kk <= qq).astype(np.float32)
    return _to_bf16(mp.transpose(0, 2, 1, 3).reshape(NQT, 128, 8 * 512))


def kernel(L0_attn_x, L0_attn_Wq, L0_attn_Wk, L0_attn_Wv, L0_attn_Wo,
           L0_ffn_w_fc, L0_ffn_w_proj):
    X = np.ascontiguousarray(np.asarray(L0_attn_x, np.float32))
    Wq = np.asarray(L0_attn_Wq, np.float32)
    Wk = np.asarray(L0_attn_Wk, np.float32)
    Wv = np.asarray(L0_attn_Wv, np.float32)
    Wo = np.asarray(L0_attn_Wo, np.float32)
    Wfc = np.asarray(L0_ffn_w_fc, np.float32)
    Wpj = np.asarray(L0_ffn_w_proj, np.float32)

    wq_t = np.ascontiguousarray(Wq.reshape(16, 128, 16, 128).transpose(3, 0, 2, 1)
                                .reshape(128, 16 * D))
    wk_t = np.ascontiguousarray(Wk.reshape(16, 128, 16, 128).transpose(3, 0, 2, 1)
                                .reshape(128, 16 * D))
    wv_t = np.ascontiguousarray(Wv.reshape(4, 512, 16, 128).transpose(3, 0, 2, 1)
                                .reshape(128, 4 * FF))
    wo_t = np.ascontiguousarray(Wo.reshape(4, 512, 16, 128).transpose(3, 2, 0, 1)
                                .reshape(128, 64 * 512))
    wfc_t = np.ascontiguousarray(Wfc.T.reshape(16, 128, FF))
    wpj_t = np.ascontiguousarray(Wpj.T.reshape(64, 128, D))

    rk_sin, rk_cos = _rope_tables(np.arange(S))

    in_maps, meta = [], []
    for c in range(8):
        b, half = c // 2, c % 2
        blocks = BLOCKS_EVEN if half == 0 else BLOCKS_ODD
        own_rows = np.concatenate([np.arange(blk * 512, (blk + 1) * 512) for blk in blocks])
        qs, qc = _rope_tables(own_rows)
        in_maps.append({
            "x_full": X[b],
            "x_own": np.ascontiguousarray(X[b][own_rows]),
            "wq": wq_t, "wk": wk_t, "wv": wv_t, "wo": wo_t,
            "wfc": wfc_t, "wpj": wpj_t,
            "rk_sin": rk_sin, "rk_cos": rk_cos,
            "rq_sin": qs, "rq_cos": qc,
            "maskpack": _maskpack_for(blocks),
        })
        meta.append((b, own_rows))

    nc = build_nc()
    res = run_bass_kernel_spmd(nc, in_maps, core_ids=list(range(8)))

    Y = np.empty((B, S, D), dtype=np.float32)
    for c in range(8):
        b, own_rows = meta[c]
        Y[b][own_rows] = res.results[c]["out"]
    return Y
